# version 01
# baseline (speedup 1.0000x reference)
import numpy as np
import ml_dtypes
import concourse.bass as bass
import concourse.mybir as mybir
from concourse.bass_utils import run_bass_kernel_spmd
from contextlib import ExitStack
import bisect

F32 = mybir.dt.float32
F16 = mybir.dt.float16
BF16 = mybir.dt.bfloat16
I32 = mybir.dt.int32
FP8 = mybir.dt.float8e4
AF = mybir.ActivationFunctionType
ALU = mybir.AluOpType
AX = mybir.AxisListType

T = 4096
D = 2048
NDB = 16
HID = 5632
NHB = 44
NH = 16
DH = 128
R = 128
NS = 128
SCAN_TC = 1024
NCHUNK = T // SCAN_TC
NTS = T // 512

STREAMS = ("tensor", "vector", "scalar", "gpsimd", "sync")
QUEUES = {"qs": "sync", "qa": "scalar", "qg": "gpsimd"}
SAME_ENGINE_SYNC = False
POOL_MOD = 2


class Op:
    __slots__ = ("fn", "marked", "waits", "queue")

    def __init__(self, fn, queue):
        self.fn = fn
        self.waits = []
        self.marked = queue is not None
        self.queue = queue


NQSEM = 8


class Builder:
    def __init__(self):
        self.ops = {s: [] for s in STREAMS}
        self.marks = {s: [] for s in STREAMS}
        self.qcount = {q: 0 for q in QUEUES}
        self.scount = {}
        self.qpos = {}
        self.bufs = {}

    def _is_dma(self, stream, idx):
        return (stream, idx) in self.qpos

    def _dep(self, owner, idx):
        q = self.qpos.get((owner, idx))
        if q is not None:
            return q[0], q[1] * 16
        marks = self.marks[owner]
        i = bisect.bisect_left(marks, idx)
        if i < len(marks):
            return owner, i + 1
        self.ops[owner][idx].marked = True
        marks.append(idx)
        return owner, len(marks)

    def emit(self, stream, fn, reads=(), writes=(), queue=None, sync_self=False):
        idx = len(self.ops[stream])
        op = Op(fn, queue)
        self.ops[stream].append(op)
        if queue is not None:
            semkey = (queue, self.qcount[queue] % NQSEM)
            self.qcount[queue] += 1
            n = self.scount.get(semkey, 0) + 1
            self.scount[semkey] = n
            self.qpos[(stream, idx)] = (semkey, n)
            op.queue = semkey
            if n > 1:
                op.waits.append((semkey, (n - 1) * 16))
        deps = {}
        xeng = queue is not None or SAME_ENGINE_SYNC or sync_self
        for r in reads:
            st = self.bufs.get(r)
            if st is None:
                continue
            lw = st[0]
            if lw is not None and (lw[0] != stream or self._is_dma(*lw) or xeng):
                deps[lw] = True
        for w in writes:
            st = self.bufs.get(w)
            if st is not None:
                lw = st[0]
                if lw is not None and (lw[0] != stream or self._is_dma(*lw) or xeng):
                    deps[lw] = True
                for rd in st[1]:
                    if rd[0] != stream or self._is_dma(*rd) or xeng:
                        deps[rd] = True
        deps.pop((stream, idx), None)
        for (o, i) in deps:
            op.waits.append(self._dep(o, i))
        for r in reads:
            self.bufs.setdefault(r, [None, []])[1].append((stream, idx))
        for w in writes:
            self.bufs[w] = [(stream, idx), []]
        return (stream, idx)

    def barrier(self):
        deps = []
        for s in STREAMS:
            if self.ops[s]:
                deps.append(self._dep(s, len(self.ops[s]) - 1))
        for semkey, n in self.scount.items():
            deps.append((semkey, n * 16))
        for s in STREAMS:
            op = Op(lambda e: e.nop(), None)
            op.waits = list(deps)
            self.ops[s].append(op)

    def final_wait(self):
        deps = [(semkey, n * 16) for semkey, n in self.scount.items()]
        for s in STREAMS:
            op = Op(lambda e: e.nop(), None)
            op.waits = list(deps)
            self.ops[s].append(op)

    def finalize(self, nc, block, stack):
        semkeys = list(STREAMS) + [(q, i) for q in QUEUES for i in range(NQSEM)]
        sems = {k: stack.enter_context(nc.semaphore(f"sem_{k}")) for k in semkeys}

        def run_stream(name):
            def body(eng):
                waited = {}
                for op in self.ops[name]:
                    for semkey, val in op.waits:
                        if waited.get(semkey, 0) >= val:
                            continue
                        eng.wait_ge(sems[semkey], val)
                        waited[semkey] = val
                    if op.fn is None:
                        continue
                    inst = op.fn(eng)
                    if op.queue is not None:
                        inst.then_inc(sems[op.queue], 16)
                    elif op.marked:
                        inst.then_inc(sems[name], 1)
            return body

        block.sync(run_stream("sync"))
        block.tensor(run_stream("tensor"))
        block.vector(run_stream("vector"))
        block.scalar(run_stream("scalar"))
        block.gpsimd(run_stream("gpsimd"))

    def dma(self, queue, out, in_, reads=(), writes=()):
        return self.emit(QUEUES[queue], lambda e, o=out, i=in_: e.dma_start(o, i),
                         reads=reads, writes=writes, queue=queue)


def build_program(debug_stage=0):
    nc = bass.Bass("TRN2", target_bir_lowering=False)
    b = Builder()

    def dram(name, shape, dt, kind="Internal"):
        return nc.dram_tensor(name, list(shape), dt, kind=kind).ap()

    xT = dram("xT", (D, T), F32, "ExternalInput")
    wq_t = dram("wq_t", (NDB, NDB, 128, 128), F32, "ExternalInput")
    wk_t = dram("wk_t", (NDB, NDB, 128, 128), F32, "ExternalInput")
    wv_t = dram("wv_t", (NDB, NDB, 128, 128), F32, "ExternalInput")
    wo_t = dram("wo_t", (NDB, NDB, 128, 128), F32, "ExternalInput")
    wsel_t = dram("wsel_t", (NDB, NDB, 128, 128), F32, "ExternalInput")
    wout_t = dram("wout_t", (NDB, NDB, 128, 128), F32, "ExternalInput")
    wxp_t = dram("wxp_t", (NDB, 3, 128, 128), F32, "ExternalInput")
    wdt_t = dram("wdt_t", (1, NDB, 128, 128), F32, "ExternalInput")
    w1T = dram("w1T", (D, HID), F32, "ExternalInput")
    w2T = dram("w2T", (D, HID), F32, "ExternalInput")
    w3T = dram("w3T", (HID, D), F32, "ExternalInput")
    alogP = dram("alogP", (128, NDB * NS), F32, "ExternalInput")
    vecs = dram("vecs", (128, 9 * NDB), F32, "ExternalInput")
    consts_in = dram("consts_in", (128, 4 * 512 + 128), BF16, "ExternalInput")

    outT = dram("outT", (D, T), F32, "ExternalOutput")
    dbg_g = dram("dbg_g", (128, 4), F32,
                 "ExternalOutput" if debug_stage in (2, 4) else "Internal")
    dbg_m = dram("dbg_m", (128, 4 * 512), BF16,
                 "ExternalOutput" if debug_stage == 1 else "Internal")
    dbg_e = dram("dbg_e", (128, 4 * 512), BF16,
                 "ExternalOutput" if debug_stage == 1 else "Internal")
    x1T = dram("x1T", (D, T), F32, "ExternalOutput" if debug_stage == 1 else "Internal")
    x2T = dram("x2T", (D, T), F32, "ExternalOutput" if debug_stage in (2, 4) else "Internal")
    x3T = dram("x3T", (D, T), F32, "ExternalOutput" if debug_stage == 3 else "Internal")
    ctx_s = dram("ctx_s", (NH, 128, T), BF16,
                 "ExternalOutput" if debug_stage == 1 else "Internal")
    hid_s = dram("hid_s", (NHB, 128, T), BF16,
                 "ExternalOutput" if debug_stage in (2, 4) else "Internal")
    w1q_s = dram("w1q_s", (NHB, NDB, 128, 128), FP8,
                 "ExternalOutput" if debug_stage in (2, 4) else "Internal")
    w2q_s = dram("w2q_s", (NHB, NDB, 128, 128), FP8,
                 "ExternalOutput" if debug_stage in (2, 4) else "Internal")
    w3q_s = dram("w3q_s", (NDB, NHB, 128, 128), FP8)
    dt_s = dram("dt_s", (NDB, 128, T), F16)
    wssm_s = dram("wssm_s", (NDB, 128, T), F16)
    ln2_s = dram("ln2_s", (NDB, 128, T), BF16)
    Brow_s = dram("Brow_s", (NS, T), F16)
    Crow_s = dram("Crow_s", (NS, T), F16)

    stack = ExitStack()

    def sb(name, p, f, dt, st=None):
        return (st or stack).enter_context(nc.sbuf_tensor(name, [p, f], dt))

    ones_col = sb("ones_col", 128, 1, BF16)
    ones_row = sb("ones_row", 1, 128, BF16)
    ones_col16 = sb("ones_col16", 128, 1, F16)
    ones_row16 = sb("ones_row16", 1, 128, F16)
    ident = sb("ident", 128, 128, BF16)
    ident16 = sb("ident16", 128, 128, F16)
    masks = sb("masks", 128, 4 * 512, BF16)
    vec_sb = sb("vec_sb", 128, 9 * NDB, F32)
    bg_sb = sb("bg_sb", 128, NDB, F32)
    A_sb = sb("A_sb", 128, NDB * NS, F32)
    row1 = sb("row1", 1, 512, F32)
    row2 = sb("row2", 1, 512, F32)
    row3 = sb("row3", 1, 512, F32)
    rowb = sb("rowb", 1, 512, F16)
    sc1 = sb("sc1", 1, 129, F32)
    onesf = sb("onesf", 128, 1, F32)
    onesrf = sb("onesrf", 1, 128, F32)
    sgam = sb("sgam", 128, 4, F32)
    cst = sb("cst", 128, 4, F32)

    pb = [stack.enter_context(nc.psum_tensor(f"pb{i}", [128, 512], F32))
          for i in range(8)]
    PB = [("pb", i) for i in range(8)]

    def mm(lhsT_ap, rhs_ap, out_ap, start, stop, reads, writes):
        b.emit("tensor", lambda e, l=lhsT_ap, r=rhs_ap, o=out_ap, s1=start, s2=stop:
               e.matmul(o, l, r, start=s1, stop=s2), reads=reads, writes=writes)

    act = lambda fn, reads, writes, ss=False: b.emit("scalar", fn, reads=reads,
                                                    writes=writes, sync_self=ss)
    vec = lambda fn, reads, writes, ss=False: b.emit("vector", fn, reads=reads,
                                                    writes=writes, sync_self=ss)
    pool = lambda fn, reads, writes, ss=False: b.emit("gpsimd", fn, reads=reads,
                                                     writes=writes, sync_self=ss)

    def vcol(v, j):
        return vec_sb[:, v * NDB + j: v * NDB + j + 1]

    b.dma("qs", vec_sb[:], vecs, writes=[("vec_sb",)])
    pool(lambda e: e.memset(ones_col[:], 1.0), [], [("ones_col",)])
    pool(lambda e: e.memset(ones_row[:], 1.0), [], [("ones_row",)])
    pool(lambda e: e.memset(ones_col16[:], 1.0), [], [("ones_col",)])
    pool(lambda e: e.memset(ones_row16[:], 1.0), [], [("ones_row",)])
    pool(lambda e: e.memset(onesf[:], 1.0), [], [("onesf",)])
    pool(lambda e: e.memset(onesrf[:], 1.0), [], [("onesrf",)])
    pool(lambda e: e.memset(cst[:, 0:1], -4.0), [], [("cst",)])
    pool(lambda e: e.memset(cst[:, 1:2], 1e-5), [], [("cst",)])
    pool(lambda e: e.memset(cst[:, 2:3], 1e-6), [], [("cst",)])
    pool(lambda e: e.memset(cst[:, 3:4], 1.0), [], [("cst",)])
    b.dma("qs", masks[:], consts_in[:, 0:4 * 512], writes=[("masks",)])
    b.dma("qs", ident[:], consts_in[:, 4 * 512:4 * 512 + 128], writes=[("ident",)])
    vec(lambda e: e.tensor_copy(ident16[:], ident[:]), [("ident",)],
        [("ident",)])
    vec(lambda e: e.tensor_tensor(bg_sb[:], vec_sb[:, 5 * NDB:6 * NDB],
                                  vec_sb[:, 6 * NDB:7 * NDB], ALU.mult),
        [("vec_sb",)], [("bg_sb",)], ss=True)
    b.dma("qs", A_sb[:], alogP, writes=[("A_sb",)])
    act(lambda e: e.activation(A_sb[:], A_sb[:], AF.Exp), [("A_sb",)], [("A_sb",)])
    vec(lambda e: e.tensor_scalar(A_sb[:], A_sb[:], -1.0, None, ALU.mult),
        [("A_sb",)], [("A_sb",)], ss=True)

    with ExitStack() as qst:
        wrow = [sb(f"wrow{i}", 128, HID, F32, qst) for i in range(2)]
        wpart = sb("wpart", 128, 4, F32, qst)
        wq8 = sb("wq8", 128, HID, FP8, qst)
        wqi = sb("wqi", 128, HID, I32, qst)
        specs = [(w1T, w1q_s, NDB, HID, NHB), (w2T, w2q_s, NDB, HID, NHB),
                 (w3T, w3q_s, NHB, D, NDB)]
        for mi, (wsrc, qdst, nrows, rowlen, nblk) in enumerate(specs):
            for ki in range(nrows):
                buf = wrow[ki % 2]
                bk = ("wrow", ki % 2)
                b.dma("qs", buf[:, 0:rowlen], wsrc[ki * 128:(ki + 1) * 128, :],
                      writes=[bk])
                if ki == 0:
                    vec(lambda e, bf=buf, L=rowlen, mi=mi: e.tensor_reduce(
                        wpart[:, mi:mi + 1], bf[:, 0:L], axis=AX.X, op=ALU.add,
                        apply_absolute_value=True), [bk], [("wpart", mi)])
                else:
                    vec(lambda e, bf=buf, L=rowlen: e.tensor_reduce(
                        wpart[:, 3:4], bf[:, 0:L], axis=AX.X, op=ALU.add,
                        apply_absolute_value=True), [bk], [("wpart", 3)])
                    vec(lambda e, mi=mi: e.tensor_tensor(
                        wpart[:, mi:mi + 1], wpart[:, mi:mi + 1], wpart[:, 3:4],
                        ALU.add), [("wpart", 3), ("wpart", mi)], [("wpart", mi)],
                        ss=True)
        CNT = float(HID * D)
        gsc = dram("gam_scratch", (4, 128), F32)
        grb = sb("grb", 128, 128, F32, qst)
        for mi in range(3):
            b.dma("qs", gsc[mi:mi + 1, :], wpart[:, mi:mi + 1],
                  reads=[("wpart", mi)], writes=[("gsc", mi)])
            b.dma("qs", grb[:], gsc[mi:mi + 1, :].broadcast_to([128, 128]),
                  reads=[("gsc", mi)], writes=[("grb",)])
            vec(lambda e, mi=mi: e.tensor_reduce(sgam[:, mi:mi + 1], grb[:],
                                                 axis=AX.X, op=ALU.add),
                [("grb",)], [("sgam", mi)], ss=True)
            vec(lambda e, mi=mi: e.tensor_scalar_add(
                sgam[:, mi:mi + 1], sgam[:, mi:mi + 1], CNT * 1e-8),
                [("sgam", mi)], [("sgam", mi)], ss=True)
            vec(lambda e, mi=mi: e.reciprocal(sgam[:, mi:mi + 1], sgam[:, mi:mi + 1]),
                [("sgam", mi)], [("sgam", mi)], ss=True)
            vec(lambda e, mi=mi: e.tensor_scalar_mul(
                sgam[:, mi:mi + 1], sgam[:, mi:mi + 1], CNT),
                [("sgam", mi)], [("sgam", mi)], ss=True)
        for mi, (wsrc, qdst, nrows, rowlen, nblk) in enumerate(specs):
            for ki in range(nrows):
                buf = wrow[ki % 2]
                bk = ("wrow", ki % 2)
                b.dma("qs", buf[:, 0:rowlen], wsrc[ki * 128:(ki + 1) * 128, :],
                      writes=[bk])
                vec(lambda e, bf=buf, L=rowlen, mi=mi: e.tensor_scalar(
                    wqi[:, 0:L], bf[:, 0:L], sgam[:, mi:mi + 1], None, ALU.mult),
                    [bk, ("sgam", mi)], [("wqi",)])
                vec(lambda e, L=rowlen: e.tensor_scalar(
                    wq8[:, 0:L], wqi[:, 0:L], 1.0, -1.0, ALU.min, ALU.max),
                    [("wqi",)], [("wq8",)])
                for blk in range(nblk):
                    b.dma("qs", qdst[blk, ki], wq8[:, blk * 128:(blk + 1) * 128],
                          reads=[("wq8",)])
    if debug_stage == 2:
        b.dma("qs", dbg_g, sgam[:], reads=[("sgam", 0), ("sgam", 1), ("sgam", 2)])
    b.barrier()

    def norm_stage(tag, src_dram, xn, ls, rms=False):
        xk = lambda ki: (tag, ki)
        with ExitStack() as ns:
            l32 = [sb(f"{tag}_l32_{i}", 128, 512, F32, ns) for i in range(2)]
            sq = sb(f"{tag}_sq", 128, 512, BF16, ns)
            rstdB = sb(f"{tag}_rstdB", 128, T, F16, ns)
            mrB = sb(f"{tag}_mrB", 128, T, F16, ns) if not rms else None
            for ki in range(NDB):
                for ts in range(NTS):
                    buf = l32[ts % 2]
                    bk = (tag, "l32", ts % 2)
                    b.dma("qs", buf[:],
                          src_dram[ki * 128:(ki + 1) * 128, ts * 512:(ts + 1) * 512],
                          writes=[bk])
                    vec(lambda e, bf=buf, ki=ki, ts=ts: e.tensor_copy(
                        xn[:, ki * T + ts * 512: ki * T + (ts + 1) * 512], bf[:]),
                        [bk], [xk(ki)])
            for ts in range(NTS):
                c0 = ts * 512
                if not rms:
                    for ki in range(NDB):
                        mm(ones_col[:], xn[:, ki * T + c0: ki * T + c0 + 512],
                           pb[6][0:1, :], ki == 0, ki == NDB - 1,
                           [xk(ki), ("ones_col",)], [PB[6]])
                for ki in range(NDB):
                    act(lambda e, ki=ki, c0=c0: e.square(
                        sq[:], xn[:, ki * T + c0: ki * T + c0 + 512]),
                        [xk(ki)], [(tag, "sq")])
                    mm(ones_col[:], sq[:], pb[7][0:1, :], ki == 0, ki == NDB - 1,
                       [(tag, "sq"), ("ones_col",)], [PB[7]])
                if rms:
                    act(lambda e: e.activation(row2[:], pb[7][0:1, :], AF.Ln,
                                               bias=cst[0:1, 2:3], scale=1.0 / D),
                        [PB[7], ("cst",)], [("row2",)])
                    act(lambda e: e.activation(row2[:], row2[:], AF.Exp, scale=-0.5),
                        [("row2",)], [("row2",)])
                else:
                    act(lambda e: e.copy(row1[:], pb[6][0:1, :]), [PB[6]], [("row1",)])
                    vec(lambda e: e.tensor_scalar(row1[:], row1[:], 1.0 / D, None,
                                                  ALU.mult), [("row1",)], [("row1",)], ss=True)
                    act(lambda e: e.copy(row2[:], pb[7][0:1, :]), [PB[7]], [("row2",)])
                    vec(lambda e: e.tensor_scalar(row2[:], row2[:], 1.0 / D, None,
                                                  ALU.mult), [("row2",)], [("row2",)], ss=True)
                    vec(lambda e: e.tensor_tensor(row3[:], row1[:], row1[:], ALU.mult),
                        [("row1",)], [("row3",)], ss=True)
                    vec(lambda e: e.tensor_tensor(row2[:], row2[:], row3[:],
                                                  ALU.subtract),
                        [("row2",), ("row3",)], [("row2",)], ss=True)
                    act(lambda e: e.activation(row2[:], row2[:], AF.Ln, bias=cst[0:1, 1:2]),
                        [("row2",), ("cst",)], [("row2",)])
                    act(lambda e: e.activation(row2[:], row2[:], AF.Exp, scale=-0.5),
                        [("row2",)], [("row2",)])
                    vec(lambda e: e.tensor_tensor(row3[:], row1[:], row2[:], ALU.mult),
                        [("row1",), ("row2",)], [("row3",)], ss=True)
                vec(lambda e: e.tensor_copy(rowb[:], row2[:]), [("row2",)], [("rowb",)], ss=True)
                mm(ones_row16[:], rowb[:], pb[6][:, :], True, True,
                   [("rowb",), ("ones_row",)], [PB[6]])
                act(lambda e, c0=c0: e.copy(rstdB[:, c0:c0 + 512], pb[6][:, :]),
                    [PB[6]], [(tag, "rstdB", ts)])
                if not rms:
                    vec(lambda e: e.tensor_copy(rowb[:], row3[:]), [("row3",)], [("rowb",)], ss=True)
                    mm(ones_row16[:], rowb[:], pb[7][:, :], True, True,
                       [("rowb",), ("ones_row",)], [PB[7]])
                    act(lambda e, c0=c0: e.copy(mrB[:, c0:c0 + 512], pb[7][:, :]),
                        [PB[7]], [(tag, "mrB", ts)])
            rB = [(tag, "rstdB", t) for t in range(NTS)]
            mB = [(tag, "mrB", t) for t in range(NTS)]
            for ki in range(NDB):
                s = xn[:, ki * T:(ki + 1) * T]
                if rms:
                    vec(lambda e, s=s, ki=ki: e.scalar_tensor_tensor(
                        s, s, vcol(4, ki), rstdB[:], ALU.mult, ALU.mult),
                        [xk(ki), ("vec_sb",)] + rB, [xk(ki)])
                else:
                    vec(lambda e, s=s: e.tensor_tensor(s, s, rstdB[:], ALU.mult),
                        [xk(ki)] + rB, [xk(ki)])
                    vec(lambda e, s=s: e.tensor_tensor(s, s, mrB[:], ALU.subtract),
                        [xk(ki)] + mB, [xk(ki)])
        b.barrier()

    def wload(dst_sb, src_ap, key):
        nk = src_ap.shape[1]
        b.dma("qg", dst_sb.rearrange("p (k c) -> p k c", c=128), src_ap, writes=[key])

    def stage_attn():
        with ExitStack() as ls:
            xn = sb("s1_xn", 128, NDB * T, BF16, ls)
            norm_stage("s1", xT, xn, ls, rms=False)
            xk = lambda ki: ("s1", ki)
            qT = sb("s1_qT", 128, T, BF16, ls)
            kT = sb("s1_kT", 128, T, BF16, ls)
            vtok = sb("s1_vtok", 128, T, BF16, ls)
            wh = [sb(f"s1_wh{i}", 128, D, BF16, ls) for i in range(2)]
            ex = [sb(f"s1_ex{i}", 128, 512, BF16, ls) for i in range(3)]
            recB = sb("s1_recB", 128, 512, F16, ls)
            ctxh = sb("s1_ctxh", 128, T, BF16, ls)
            dbge = sb("s1_dbge", 128, 4 * 512, BF16, ls)
            if debug_stage == 1:
                b.dma("qs", dbg_m, masks[:], reads=[("masks",)])
            wi = 0
            for h in range(NH):
                w = wh[wi % 2]; wk_ = ("wh", wi % 2); wi += 1
                wload(w[:], wq_t[:, h].rearrange("k p c -> p k c"), wk_)
                for ts in range(NTS):
                    c0 = ts * 512
                    pbi = PB[ts % 2]
                    for ki in range(NDB):
                        mm(w[:, ki * 128:(ki + 1) * 128],
                           xn[:, ki * T + c0: ki * T + c0 + 512],
                           pb[ts % 2][:], ki == 0, ki == NDB - 1,
                           [wk_, xk(ki)], [pbi])
                    vec(lambda e, c0=c0, p=ts % 2, h=h: e.tensor_scalar(
                        qT[:, c0:c0 + 512], pb[p][:], vcol(0, h), None, ALU.add),
                        [pbi, ("vec_sb",)], [("qT", ts)])
                w = wh[wi % 2]; wk_ = ("wh", wi % 2); wi += 1
                wload(w[:], wk_t[:, h].rearrange("k p c -> p k c"), wk_)
                for ts in range(NTS):
                    c0 = ts * 512
                    pbi = PB[2 + ts % 2]
                    for ki in range(NDB):
                        mm(w[:, ki * 128:(ki + 1) * 128],
                           xn[:, ki * T + c0: ki * T + c0 + 512],
                           pb[2 + ts % 2][:], ki == 0, ki == NDB - 1,
                           [wk_, xk(ki)], [pbi])
                    vec(lambda e, c0=c0, p=2 + ts % 2, h=h: e.tensor_scalar(
                        kT[:, c0:c0 + 512], pb[p][:], vcol(1, h), None, ALU.add),
                        [pbi, ("vec_sb",)], [("kT", ts)])
                w = wh[wi % 2]; wk_ = ("wh", wi % 2); wi += 1
                wload(w[:], wv_t[:, h].rearrange("k p c -> p k c"), wk_)
                for tt in range(T // 128):
                    pbi = PB[4 + tt % 2]
                    for ki in range(NDB):
                        mm(xn[:, ki * T + tt * 128: ki * T + (tt + 1) * 128],
                           w[:, ki * 128:(ki + 1) * 128],
                           pb[4 + tt % 2][:, 0:128], ki == 0, ki == NDB - 1,
                           [wk_, xk(ki)], [pbi])
                    act(lambda e, tt=tt, p=4 + tt % 2: e.copy(
                        vtok[:, tt * 128:(tt + 1) * 128], pb[p][:, 0:128]),
                        [pbi], [("vtok", tt)])
                for qt in range(NTS):
                    nk = 4 * (qt + 1)
                    for kt in range(nk):
                        pbs = PB[6 + kt % 2]
                        mm(kT[:, kt * 128:(kt + 1) * 128],
                           qT[:, qt * 512:(qt + 1) * 512],
                           pb[6 + kt % 2][:], True, True,
                           [("kT", kt // 4), ("qT", qt)], [pbs])
                        exi = kt % 3
                        act(lambda e, p=6 + kt % 2, i=exi: e.activation(
                            ex[i][:], pb[p][:], AF.Exp, bias=cst[:, 0:1],
                            scale=0.08838834764831845),
                            [pbs, ("cst",)], [("ex", exi)])
                        if kt >= nk - 4:
                            j = kt - (nk - 4)
                            vec(lambda e, i=exi, j=j: e.tensor_tensor(
                                ex[i][:], ex[i][:], masks[:, j * 512:(j + 1) * 512],
                                ALU.mult), [("ex", exi), ("masks",)], [("ex", exi)])
                        if debug_stage == 1 and h == 0 and qt == 0:
                            vec(lambda e, i=exi, kt=kt: e.tensor_copy(
                                dbge[:, kt * 512:(kt + 1) * 512], ex[i][:]),
                                [("ex", exi)], [("dbge", kt)])
                        mm(vtok[:, kt * 128:(kt + 1) * 128], ex[exi][:],
                           pb[4][:], kt == 0, kt == nk - 1,
                           [("vtok", kt), ("ex", exi)], [PB[4]])
                        mm(ones_col[:], ex[exi][:], pb[5][0:1, :], kt == 0, kt == nk - 1,
                           [("ex", exi), ("ones_col",)], [PB[5]])
                    act(lambda e: e.copy(row1[:], pb[5][0:1, :]), [PB[5]], [("row1",)])
                    vec(lambda e: e.reciprocal(row1[:], row1[:]), [("row1",)], [("row1",)], ss=True)
                    vec(lambda e: e.tensor_copy(rowb[:], row1[:]), [("row1",)], [("rowb",)], ss=True)
                    mm(ones_row16[:], rowb[:], pb[5][:, :], True, True,
                       [("rowb",), ("ones_row",)], [PB[5]])
                    act(lambda e: e.copy(recB[:], pb[5][:, :]), [PB[5]], [("recB",)])
                    vec(lambda e, qt=qt: e.tensor_tensor(
                        ctxh[:, qt * 512:(qt + 1) * 512], pb[4][:], recB[:], ALU.mult),
                        [PB[4], ("recB",)], [("ctxh", qt)])
                    vec(lambda e, qt=qt, h=h: e.tensor_scalar(
                        ctxh[:, qt * 512:(qt + 1) * 512],
                        ctxh[:, qt * 512:(qt + 1) * 512], vcol(2, h), None, ALU.add),
                        [("ctxh", qt), ("vec_sb",)], [("ctxh", qt)])
                b.dma("qs", ctx_s[h], ctxh[:],
                      reads=[("ctxh", q) for q in range(NTS)], writes=[("ctx_dram", h)])
                if debug_stage == 1 and h == 0:
                    b.dma("qs", dbg_e, dbge[:], reads=[("dbge", k_) for k_ in range(4)])
        b.barrier()
        with ExitStack() as ls:
            cx = sb("s1_cx", 128, NH * 512, BF16, ls)
            wo_d = [sb(f"s1_wo{i}", 128, D, BF16, ls) for i in range(2)]
            xres = [sb(f"s1_xr{i}", 128, 512, F32, ls) for i in range(2)]
            o32 = [sb(f"s1_o32{i}", 128, 512, F32, ls) for i in range(2)]
            for ts in range(NTS):
                c0 = ts * 512
                for hk in range(NH):
                    b.dma("qs", cx[:, hk * 512:(hk + 1) * 512],
                          ctx_s[hk, :, c0:c0 + 512],
                          reads=[("ctx_dram", hk)], writes=[("cx", hk)])
                for dot in range(NDB):
                    w = wo_d[dot % 2]
                    wko = ("wo_d", dot % 2)
                    wload(w[:], wo_t[:, dot].rearrange("k p c -> p k c"), wko)
                    pbi = PB[dot % 2]
                    for hk in range(NH):
                        mm(w[:, hk * 128:(hk + 1) * 128], cx[:, hk * 512:(hk + 1) * 512],
                           pb[dot % 2][:], hk == 0, hk == NH - 1,
                           [wko, ("cx", hk)], [pbi])
                    xr = xres[dot % 2]
                    b.dma("qs", xr[:], xT[dot * 128:(dot + 1) * 128, c0:c0 + 512],
                          writes=[("s1xr", dot % 2)])
                    ob = o32[dot % 2]
                    vec(lambda e, p=dot % 2, dot=dot, ob=ob, xr=xr:
                        e.scalar_tensor_tensor(ob[:], pb[p][:], vcol(3, dot), xr[:],
                                               ALU.add, ALU.add),
                        [pbi, ("s1xr", dot % 2), ("vec_sb",)], [("s1o", dot % 2)])
                    b.dma("qs", x1T[dot * 128:(dot + 1) * 128, c0:c0 + 512], ob[:],
                          reads=[("s1o", dot % 2)], writes=[("x1T", dot, ts)])

    def stage_ffn(tag, src_dram, dst_dram):
        with ExitStack() as ls:
            xn = sb(f"{tag}_xn", 128, NDB * T, BF16, ls)
            norm_stage(tag, src_dram, xn, ls, rms=False)
            xk = lambda ki: (tag, ki)
            w1b = [sb(f"{tag}_w1b{i}", 128, NDB * 128, FP8, ls) for i in range(2)]
            w2b = [sb(f"{tag}_w2b{i}", 128, NDB * 128, FP8, ls) for i in range(2)]
            sig = sb(f"{tag}_sig", 128, 512, BF16, ls)
            hrow = sb(f"{tag}_hrow", 128, T, BF16, ls)
            for hs in range(NHB):
                wa, wb_ = w1b[hs % 2], w2b[hs % 2]
                b.dma("qs", wa[:].rearrange("p (k c) -> p k c", c=128),
                      w1q_s[hs].rearrange("k p c -> p k c"),
                      writes=[("w1b", hs % 2)])
                b.dma("qs", wb_[:].rearrange("p (k c) -> p k c", c=128),
                      w2q_s[hs].rearrange("k p c -> p k c"),
                      writes=[("w2b", hs % 2)])
                for ts in range(NTS):
                    c0 = ts * 512
                    p1, p2 = ts % 2, 2 + ts % 2
                    for ki in range(NDB):
                        mm(wa[:, ki * 128:(ki + 1) * 128],
                           xn[:, ki * T + c0:ki * T + c0 + 512],
                           pb[p1][:], ki == 0, ki == NDB - 1,
                           [("w1b", hs % 2), xk(ki)], [PB[p1]])
                    for ki in range(NDB):
                        mm(wb_[:, ki * 128:(ki + 1) * 128],
                           xn[:, ki * T + c0:ki * T + c0 + 512],
                           pb[p2][:], ki == 0, ki == NDB - 1,
                           [("w2b", hs % 2), xk(ki)], [PB[p2]])
                    act(lambda e, p=p1: e.activation(sig[:], pb[p][:], AF.Sigmoid),
                        [PB[p1]], [(tag, "sig")])
                    vec(lambda e, p=p2, c0=c0: e.tensor_tensor(
                        hrow[:, c0:c0 + 512], pb[p][:], sig[:], ALU.mult),
                        [PB[p2], (tag, "sig")], [("hrow", ts)])
                b.dma("qs", hid_s[hs], hrow[:],
                      reads=[("hrow", t) for t in range(NTS)], writes=[("hid_dram", hs)])
        b.barrier()
        with ExitStack() as ls:
            w3r = [sb(f"{tag}_w3r{i}", 128, NHB * 128, FP8, ls) for i in range(2)]
            hcol = sb(f"{tag}_hcol", 128, NHB * 512, BF16, ls)
            xres = [sb(f"{tag}_xr{i}", 128, 512, F32, ls) for i in range(2)]
            o32 = [sb(f"{tag}_o32{i}", 128, 512, F32, ls) for i in range(2)]
            for ts in range(NTS):
                c0 = ts * 512
                for hk in range(NHB):
                    b.dma("qs", hcol[:, hk * 512:(hk + 1) * 512],
                          hid_s[hk, :, c0:c0 + 512],
                          reads=[("hid_dram", hk)], writes=[("hcol", hk)])
                for dot in range(NDB):
                    w3 = w3r[dot % 2]
                    b.dma("qs", w3[:].rearrange("p (k c) -> p k c", c=128),
                          w3q_s[dot].rearrange("k p c -> p k c"),
                          writes=[("w3r", dot % 2)])
                    pbi = PB[4 + dot % 2]
                    for hk in range(NHB):
                        mm(w3[:, hk * 128:(hk + 1) * 128],
                           hcol[:, hk * 512:(hk + 1) * 512],
                           pb[4 + dot % 2][:], hk == 0, hk == NHB - 1,
                           [("w3r", dot % 2), ("hcol", hk)], [pbi])
                    xr = xres[dot % 2]
                    b.dma("qs", xr[:], src_dram[dot * 128:(dot + 1) * 128, c0:c0 + 512],
                          writes=[(tag, "xr", dot % 2)])
                    ob = o32[dot % 2]
                    vec(lambda e, p=4 + dot % 2, ob=ob, xr=xr: e.tensor_tensor(
                        ob[:], pb[p][:], xr[:], ALU.add),
                        [pbi, (tag, "xr", dot % 2)], [(tag, "o", dot % 2)])
                    b.dma("qs", dst_dram[dot * 128:(dot + 1) * 128, c0:c0 + 512], ob[:],
                          reads=[(tag, "o", dot % 2)], writes=[("dst", tag, dot, ts)])

    def stage_ssm():
        with ExitStack() as ls:
            xn = sb("s3_xn", 128, NDB * T, BF16, ls)
            norm_stage("s3", x2T, xn, ls, rms=False)
            xk = lambda ki: ("s3", ki)
            for ki in range(NDB):
                b.dma("qs", ln2_s[ki], xn[:, ki * T:(ki + 1) * T],
                      reads=[xk(ki)], writes=[("ln2_dram", ki)])
            with ExitStack() as rs:
                sq2 = sb("s3_sq2", 128, 512, BF16, rs)
                rstdB2 = sb("s3_rstdB2", 128, T, F16, rs)
                for ts in range(NTS):
                    c0 = ts * 512
                    for ki in range(NDB):
                        act(lambda e, ki=ki, c0=c0: e.square(
                            sq2[:], xn[:, ki * T + c0: ki * T + c0 + 512]),
                            [xk(ki)], [("s3sq2",)])
                        mm(ones_col[:], sq2[:], pb[7][0:1, :], ki == 0, ki == NDB - 1,
                           [("s3sq2",), ("ones_col",)], [PB[7]])
                    act(lambda e: e.activation(row2[:], pb[7][0:1, :], AF.Ln,
                                               bias=cst[0:1, 2:3], scale=1.0 / D),
                        [PB[7], ("cst",)], [("row2",)])
                    act(lambda e: e.activation(row2[:], row2[:], AF.Exp, scale=-0.5),
                        [("row2",)], [("row2",)])
                    vec(lambda e: e.tensor_copy(rowb[:], row2[:]), [("row2",)],
                        [("rowb",)])
                    mm(ones_row16[:], rowb[:], pb[6][:, :], True, True,
                       [("rowb",), ("ones_row",)], [PB[6]])
                    act(lambda e, c0=c0: e.copy(rstdB2[:, c0:c0 + 512], pb[6][:, :]),
                        [PB[6]], [("s3rB", ts)])
                rB2 = [("s3rB", t) for t in range(NTS)]
                for ki in range(NDB):
                    s = xn[:, ki * T:(ki + 1) * T]
                    vec(lambda e, s=s, ki=ki: e.scalar_tensor_tensor(
                        s, s, vcol(4, ki), rstdB2[:], ALU.mult, ALU.mult),
                        [xk(ki), ("vec_sb",), ("ln2_dram", ki)] + rB2, [xk(ki)])
            b.barrier()
            with ExitStack() as ms:
                wsel_d = [sb(f"s3_wsel{i}", 128, D, BF16, ms) for i in range(2)]
                msk = sb("s3_msk", 128, NDB * 512, F16, ms)
                for ts in range(NTS):
                    c0 = ts * 512
                    for dot in range(NDB):
                        w = wsel_d[dot % 2]
                        wks = ("wsel", dot % 2)
                        wload(w[:], wsel_t[:, dot].rearrange("k p c -> p k c"), wks)
                        pbi = PB[dot % 2]
                        for ki in range(NDB):
                            mm(w[:, ki * 128:(ki + 1) * 128],
                               xn[:, ki * T + c0:ki * T + c0 + 512],
                               pb[dot % 2][:], ki == 0, ki == NDB - 1,
                               [wks, xk(ki)], [pbi])
                        act(lambda e, p=dot % 2, dot=dot: e.activation(
                            msk[:, dot * 512:(dot + 1) * 512], pb[p][:], AF.Sigmoid,
                            bias=bg_sb[:, dot:dot + 1], scale=vcol(6, dot)),
                            [pbi, ("bg_sb",), ("vec_sb",)], [("msk", dot)])
                    for dot in range(NDB):
                        vec(lambda e, dot=dot, c0=c0: e.tensor_tensor(
                            xn[:, dot * T + c0:dot * T + c0 + 512],
                            xn[:, dot * T + c0:dot * T + c0 + 512],
                            msk[:, dot * 512:(dot + 1) * 512], ALU.mult),
                            [xk(dot), ("msk", dot)], [xk(dot)])
            b.barrier()
            with ExitStack() as psx:
                wxp_d = sb("s3_wxp", 128, D, BF16, psx)
                din = sb("s3_din", 128, T, BF16, psx)
                stg = sb("s3_stg", 128, 512, F16, psx)
                for m, dst_dram in [(1, Brow_s), (2, Crow_s), (0, None)]:
                    wload(wxp_d[:], wxp_t[:, m].rearrange("k p c -> p k c"),
                          ("wxp", m))
                    for ts in range(NTS):
                        c0 = ts * 512
                        pbi = PB[2 + ts % 2]
                        for ki in range(NDB):
                            mm(wxp_d[:, ki * 128:(ki + 1) * 128],
                               xn[:, ki * T + c0:ki * T + c0 + 512],
                               pb[2 + ts % 2][:], ki == 0, ki == NDB - 1,
                               [("wxp", m), xk(ki)], [pbi])
                        if m == 0:
                            act(lambda e, p=2 + ts % 2, c0=c0: e.copy(
                                din[:, c0:c0 + 512], pb[p][:]),
                                [pbi], [("din", ts)])
                        else:
                            act(lambda e, p=2 + ts % 2: e.copy(stg[:], pb[p][:]),
                                [pbi], [("stg",)])
                            b.dma("qs", dst_dram[:, c0:c0 + 512], stg[:],
                                  reads=[("stg",)],
                                  writes=[("Brow",) if m == 1 else ("Crow",)])
                wdt_d = sb("s3_wdt", 128, D, BF16, psx)
                t1 = sb("s3_t1", 128, 512, F32, psx)
                t2 = sb("s3_t2", 128, 512, F32, psx)
                dtt = sb("s3_dtt", 128, 512, F16, psx)
                wst = sb("s3_wst", 128, 512, F16, psx)
                wload(wdt_d[:], wdt_t[0].rearrange("k p c -> p k c"), ("wdt",))
                for ts in range(NTS):
                    c0 = ts * 512
                    for dot in range(NDB):
                        pbi = PB[4 + dot % 2]
                        mm(wdt_d[:, dot * 128:(dot + 1) * 128], din[:, c0:c0 + 512],
                           pb[4 + dot % 2][:], True, True,
                           [("wdt",), ("din", ts)], [pbi])
                        act(lambda e, p=4 + dot % 2, dot=dot: e.activation(
                            t1[:], pb[p][:], AF.Abs, bias=vcol(7, dot)),
                            [pbi, ("vec_sb",)], [("s3t1",)])
                        act(lambda e: e.activation(t1[:], t1[:], AF.Exp, scale=-1.0),
                            [("s3t1",)], [("s3t1",)])
                        act(lambda e: e.activation(t1[:], t1[:], AF.Ln, bias=cst[:, 3:4]),
                            [("s3t1",), ("cst",)], [("s3t1",)])
                        act(lambda e, p=4 + dot % 2, dot=dot: e.activation(
                            t2[:], pb[p][:], AF.Relu, bias=vcol(7, dot)),
                            [pbi, ("vec_sb",)], [("s3t2",)])
                        vec(lambda e: e.tensor_tensor(dtt[:], t1[:], t2[:], ALU.add),
                            [("s3t1",), ("s3t2",)], [("s3dt",)])
                        b.dma("qs", dt_s[dot, :, c0:c0 + 512], dtt[:],
                              reads=[("s3dt",)], writes=[("dt_dram", dot, ts)])
                        vec(lambda e, dot=dot, c0=c0: e.tensor_tensor(
                            wst[:], dtt[:], xn[:, dot * T + c0:dot * T + c0 + 512],
                            ALU.mult), [("s3dt",), xk(dot)], [("s3ws",)])
                        b.dma("qs", wssm_s[dot, :, c0:c0 + 512], wst[:],
                              reads=[("s3ws",)], writes=[("ws_dram", dot, ts)])
        b.barrier()
        with ExitStack() as ls:
            sdt = sb("sc_dt", 128, NDB * SCAN_TC, F16, ls)
            sw = sb("sc_w", 128, NDB * SCAN_TC, F16, ls)
            hcarry = sb("sc_hc", 128, D, F32, ls)
            bb = [sb(f"sc_bb{i}", 128, SCAN_TC, F16, ls) for i in range(3)]
            cc = [sb(f"sc_cc{i}", 128, SCAN_TC, F16, ls) for i in range(3)]
            at = [sb(f"sc_a{i}", 128, SCAN_TC, F16, ls) for i in range(3)]
            btl = [sb(f"sc_b{i}", 128, SCAN_TC, F16, ls) for i in range(3)]
            ht = [sb(f"sc_h{i}", 128, SCAN_TC, F16, ls) for i in range(3)]
            mt = [sb(f"sc_m{i}", 128, SCAN_TC, F16, ls) for i in range(3)]
            yv = sb("sc_yv", 128, NDB * SCAN_TC, BF16, ls)
            x2t = [sb(f"sc_x2{i}", 128, SCAN_TC, BF16, ls) for i in range(2)]
            xd = sb("sc_xd", 128, SCAN_TC, BF16, ls)
            wou_d = [sb(f"sc_wou{i}", 128, D, BF16, ls) for i in range(2)]
            o32 = [sb(f"sc_o32{i}", 128, 512, F32, ls) for i in range(2)]
            x2r = [sb(f"sc_x2r{i}", 128, 512, F32, ls) for i in range(2)]
            vec(lambda e: e.memset(hcarry[:], 0.0), [], [("hc",)])
            gidx = 0
            for ci in range(NCHUNK):
                c0 = ci * SCAN_TC
                for db in range(NDB):
                    b.dma("qs", sdt[:, db * SCAN_TC:(db + 1) * SCAN_TC],
                          dt_s[db, :, c0:c0 + SCAN_TC],
                          reads=[("dt_dram", db, t)
                                 for t in range(c0 // 512, (c0 + SCAN_TC) // 512)],
                          writes=[("sdt", db)])
                    b.dma("qs", sw[:, db * SCAN_TC:(db + 1) * SCAN_TC],
                          wssm_s[db, :, c0:c0 + SCAN_TC],
                          reads=[("ws_dram", db, t)
                                 for t in range(c0 // 512, (c0 + SCAN_TC) // 512)],
                          writes=[("sw", db)])
                for quad in range(4):
                    dbs = [4 * quad + i for i in range(4)]
                    for n in range(NS):
                        ib = n % 3
                        b.dma("qs", bb[ib][:],
                              Brow_s[n:n + 1, c0:c0 + SCAN_TC].broadcast_to(
                                  [128, SCAN_TC]),
                              reads=[("Brow",)], writes=[("bb", ib)])
                        b.dma("qs", cc[ib][:],
                              Crow_s[n:n + 1, c0:c0 + SCAN_TC].broadcast_to(
                                  [128, SCAN_TC]),
                              reads=[("Crow",)], writes=[("cc", ib)])
                        for db in dbs:
                            gidx += 1
                            ia = gidx % 3
                            tteng = pool if (n % POOL_MOD == 0) else vec
                            col = db * NS + n
                            act(lambda e, db=db, ia=ia, col=col: e.activation(
                                at[ia][:], sdt[:, db * SCAN_TC:(db + 1) * SCAN_TC],
                                AF.Exp, scale=A_sb[:, col:col + 1]),
                                [("sdt", db), ("A_sb",)], [("at", ia)])
                            tteng(lambda e, db=db, ia=ia, ib=ib: e.tensor_tensor(
                                btl[ia][:], sw[:, db * SCAN_TC:(db + 1) * SCAN_TC],
                                bb[ib][:], ALU.mult),
                                [("sw", db), ("bb", ib)], [("btl", ia)])
                            vec(lambda e, ia=ia, col=col: e.tensor_tensor_scan(
                                ht[ia][:], at[ia][:], btl[ia][:],
                                hcarry[:, col:col + 1], ALU.mult, ALU.add),
                                [("at", ia), ("btl", ia), ("hc",)], [("ht", ia)])
                            vec(lambda e, ia=ia, col=col: e.tensor_copy(
                                hcarry[:, col:col + 1],
                                ht[ia][:, SCAN_TC - 1:SCAN_TC]),
                                [("ht", ia)], [("hc",)])
                            tteng(lambda e, ia=ia, ib=ib: e.tensor_tensor(
                                mt[ia][:], ht[ia][:], cc[ib][:], ALU.mult),
                                [("ht", ia), ("cc", ib)], [("mt", ia)])
                            pq = 2 * (db - 4 * quad)
                            mm(ident16[:], mt[ia][:, 0:512], pb[pq][:],
                               n == 0, n == NS - 1,
                               [("mt", ia), ("ident",)], [PB[pq]])
                            mm(ident16[:], mt[ia][:, 512:1024], pb[pq + 1][:],
                               n == 0, n == NS - 1,
                               [("mt", ia), ("ident",)], [PB[pq + 1]])
                    for i, db in enumerate(dbs):
                        xb = x2t[i % 2]
                        b.dma("qs", xb[:], ln2_s[db, :, c0:c0 + SCAN_TC],
                              reads=[("ln2_dram", db)], writes=[("scx2", i % 2)])
                        vec(lambda e, xb=xb, db=db: e.tensor_scalar(
                            xd[:], xb[:], vcol(8, db), None, ALU.mult),
                            [("scx2", i % 2), ("vec_sb",)], [("xd",)])
                        for half in range(2):
                            pq = 2 * i + half
                            vec(lambda e, db=db, half=half, pq=pq: e.tensor_tensor(
                                yv[:, db * SCAN_TC + half * 512:
                                   db * SCAN_TC + half * 512 + 512],
                                pb[pq][:], xd[:, half * 512:half * 512 + 512],
                                ALU.add),
                                [PB[pq], ("xd",)], [("yv", db)])
                for dot in range(NDB):
                    w = wou_d[dot % 2]
                    wko = ("wou", dot % 2)
                    wload(w[:], wout_t[:, dot].rearrange("k p c -> p k c"), wko)
                    for tsc in range(SCAN_TC // 512):
                        cc0 = c0 + tsc * 512
                        pbi = PB[dot % 2]
                        for ki in range(NDB):
                            mm(w[:, ki * 128:(ki + 1) * 128],
                               yv[:, ki * SCAN_TC + tsc * 512:
                                  ki * SCAN_TC + tsc * 512 + 512],
                               pb[dot % 2][:], ki == 0, ki == NDB - 1,
                               [wko, ("yv", ki)], [pbi])
                        xr = x2r[dot % 2]
                        b.dma("qs", xr[:], x2T[dot * 128:(dot + 1) * 128, cc0:cc0 + 512],
                              writes=[("scxr", dot % 2)])
                        ob = o32[dot % 2]
                        vec(lambda e, p=dot % 2, ob=ob, xr=xr: e.tensor_tensor(
                            ob[:], pb[p][:], xr[:], ALU.add),
                            [pbi, ("scxr", dot % 2)], [("sco", dot % 2)])
                        b.dma("qs", x3T[dot * 128:(dot + 1) * 128, cc0:cc0 + 512],
                              ob[:], reads=[("sco", dot % 2)],
                              writes=[("x3T", dot, cc0 // 512)])

    stage_attn()
    b.barrier()
    if debug_stage != 1:
        stage_ffn("s2", x1T, x2T)
        b.barrier()
        if debug_stage != 2:
            stage_ssm()
            b.barrier()
            if debug_stage != 3:
                stage_ffn("s4", x3T, outT)
    b.final_wait()
    with nc.Block() as block:
        b.finalize(nc, block, stack)
    stack.close()
    return nc


def _tile_w(w):
    wt = np.ascontiguousarray(np.asarray(w).T.astype(np.float32, copy=False))
    ki, j = wt.shape[0] // 128, wt.shape[1] // 128
    return np.ascontiguousarray(
        wt.reshape(ki, 128, j, 128).transpose(0, 2, 1, 3))


def _pack_cols(vs):
    cols = [np.asarray(v, np.float32).reshape(NDB, 128).T for v in vs]
    return np.ascontiguousarray(np.concatenate(cols, axis=1))


def make_inputs(g, cores):
    wq_t = _tile_w(g["L0_attn_a_Wq"])
    wk_t = _tile_w(g["L0_attn_a_Wk"])
    wv_t = _tile_w(g["L0_attn_a_Wv"])
    wo_t = _tile_w(g["L0_attn_a_Wo"])
    wsel_t = _tile_w(g["L0_attn_b_select_proj_w"])
    wout_t = _tile_w(g["L0_attn_b_out_proj_w"])
    wxp_t = _tile_w(g["L0_attn_b_x_proj_w"])
    wdt_t = _tile_w(g["L0_attn_b_dt_proj_w"])
    w1T = np.ascontiguousarray(np.asarray(g["L0_ffn_w1"]).T.astype(np.float32))
    w2T = np.ascontiguousarray(np.asarray(g["L0_ffn_w2"]).T.astype(np.float32))
    w3T = np.ascontiguousarray(np.asarray(g["L0_ffn_w3"]).T.astype(np.float32))
    alogP = np.ascontiguousarray(
        np.asarray(g["L0_attn_b_A_log"], dtype=np.float32)
        .reshape(NDB, 128, NS).transpose(1, 0, 2).reshape(128, NDB * NS))
    vecs = _pack_cols([g["L0_attn_a_bq"], g["L0_attn_a_bk"], g["L0_attn_a_bv"],
                       g["L0_attn_a_bo"], g["L0_attn_b_norm_w"],
                       g["L0_attn_b_select_proj_b"], g["L0_attn_b_select_gate"],
                       g["L0_attn_b_dt_proj_b"], g["L0_attn_b_D_param"]])
    masks_np = np.zeros((128, 4 * 512 + 128), np.float32)
    kk = np.arange(128)[:, None]
    qq = np.arange(512)[None, :]
    for j in range(4):
        masks_np[:, j * 512:(j + 1) * 512] = (qq - kk - 128 * j >= 0)
    masks_np[:, 4 * 512:] = np.eye(128, dtype=np.float32)
    consts_in = masks_np.astype(ml_dtypes.bfloat16)
    shared = dict(wq_t=wq_t, wk_t=wk_t, wv_t=wv_t, wo_t=wo_t, wsel_t=wsel_t,
                  wout_t=wout_t, wxp_t=wxp_t, wdt_t=wdt_t, w1T=w1T, w2T=w2T,
                  w3T=w3T, alogP=alogP, vecs=vecs, consts_in=consts_in)
    x = np.asarray(g["L0_attn_a_x"])
    in_maps = []
    for c in cores:
        m = dict(shared)
        m["xT"] = np.ascontiguousarray(x[c].T.astype(np.float32, copy=False))
        in_maps.append(m)
    return in_maps


def kernel(**inputs):
    x = np.asarray(inputs["L0_attn_a_x"])
    ncores = x.shape[0]
    in_maps = make_inputs(inputs, list(range(ncores)))
    nc = build_program(debug_stage=0)
    res = run_bass_kernel_spmd(nc, in_maps, core_ids=list(range(ncores)))
    out = np.empty((ncores, T, D), np.float32)
    for c in range(ncores):
        out[c] = np.asarray(res.results[c]["outT"]).T
    return out
